# version 08 (final)
# baseline (speedup 1.0000x reference)
import math
import numpy as np
import concourse.bass as bass
import concourse.mybir as mybir
from concourse.bass_utils import run_bass_kernel_spmd
from contextlib import ExitStack

F32 = mybir.dt.float32
BF16 = mybir.dt.bfloat16
I8 = mybir.dt.int8
AF = mybir.ActivationFunctionType
ALU = mybir.AluOpType

NCORES = 8
T = 1024
TH = 512
D = 2048
DC = 16
HD = 128
NH = 16
NKV = 4

ENGINES = ("sync", "gpsimd", "tensor", "vector", "scalar")
ATTN_SAFE = False
NOP_SPACER = False
TILE_SERIAL = False
DELAY_KV0 = False
QSER = True
QSER2 = False
DELAY_CC = False


class Sched:

    def __init__(self):
        self.ops = {e: [] for e in ENGINES}
        self.c = {k: 0 for k in ("ls", "lg", "st", "mm", "act", "dve", "pool", "cc")}

    def emit(self, eng, fn, waits=(), inc=None, amt=1):
        waits = [(s, v) for (s, v) in waits if v > 0]
        trip = None
        if inc is not None:
            self.c[inc] += amt
            trip = (inc, amt)
        self.ops[eng].append((waits, fn, trip))
        return self.c[inc] if inc is not None else None


def build_program(upto="full", dbg=False):
    nc = bass.Bass("TRN2", target_bir_lowering=False)

    xT = nc.dram_tensor("xT", [D, T], F32, kind="ExternalInput").ap()
    wgu = nc.dram_tensor("wgu", [64, 128, 4096], I8, kind="ExternalInput").ap()
    wdt = nc.dram_tensor("wdt", [16, 128, 8192], I8, kind="ExternalInput").ap()
    wqt = nc.dram_tensor("wqt", [16, 128, 2048], F32, kind="ExternalInput").ap()
    wkt = nc.dram_tensor("wkt", [4, 128, 2048], F32, kind="ExternalInput").ap()
    wvr = nc.dram_tensor("wvr", [16, 128, 512], F32, kind="ExternalInput").ap()
    wot = nc.dram_tensor("wot", [16, 128, 2048], F32, kind="ExternalInput").ap()
    sgd = nc.dram_tensor("sgd", [128, 64], F32, kind="ExternalInput").ap()
    sud = nc.dram_tensor("sud", [128, 64], F32, kind="ExternalInput").ap()
    sdd = nc.dram_tensor("sdd", [128, 16], F32, kind="ExternalInput").ap()
    cos2k_d = nc.dram_tensor("cos2k", [128, T], F32, kind="ExternalInput").ap()
    sinSk_d = nc.dram_tensor("sinSk", [128, T], F32, kind="ExternalInput").ap()
    cos2q_d = nc.dram_tensor("cos2q", [128, T], F32, kind="ExternalInput").ap()
    sinSq_d = nc.dram_tensor("sinSq", [128, T], F32, kind="ExternalInput").ap()

    yT = nc.dram_tensor("yT", [D, T], F32, kind="ExternalOutput").ap()

    if dbg:
        dbg_ln2 = nc.dram_tensor("dbg_ln2", [128, DC * T], BF16, kind="ExternalOutput").ap()
        dbg_q = nc.dram_tensor("dbg_q", [128, 16384], BF16, kind="ExternalOutput").ap()
        dbg_kvst = nc.dram_tensor("dbg_kvst", [128, 8192], BF16, kind="ExternalOutput").ap()
        dbg_kvag = nc.dram_tensor("dbg_kvag", [8192, T], BF16, kind="ExternalOutput").ap()
        dbg_O = nc.dram_tensor("dbg_O", [128, DC * T], BF16, kind="ExternalOutput").ap()
        dbg_kg = nc.dram_tensor("dbg_kg", [4, 128, 8192], BF16, kind="ExternalOutput").ap()
        dbg_q2 = nc.dram_tensor("dbg_q2", [128, 16384], BF16, kind="ExternalOutput").ap()
        dbg_vg = nc.dram_tensor("dbg_vg", [4, 128, 8192], BF16, kind="ExternalOutput").ap()
    wq_bf = nc.dram_tensor("wq_bf", [16, 128, 2048], BF16).ap()
    wo_bf = nc.dram_tensor("wo_bf", [16, 128, 2048], BF16).ap()
    kv_in = nc.dram_tensor("kv_in", [1024, T], BF16).ap()
    kv_ag = nc.dram_tensor("kv_ag", [8 * 1024, T], BF16, addr_space="Shared").ap()

    with ExitStack() as ctx:
        sbuf = lambda n, s, d: ctx.enter_context(nc.sbuf_tensor(n, s, d))
        psumt = lambda n, s, d: ctx.enter_context(nc.psum_tensor(n, s, d))

        x_sb = sbuf("x_sb", [128, DC * T], F32)
        ln_sb = sbuf("ln_sb", [128, DC * T], BF16)
        C = sbuf("C", [128, 16384], BF16)
        kvK = sbuf("kvK", [128, 8192], BF16)
        vg = sbuf("vg", [128, 8192], BF16)
        wgu_sb = sbuf("wgu_sb", [128, 8192], BF16)
        wdw = sbuf("wdw", [128, 4096], BF16)
        tbl_cos = sbuf("tbl_cos", [128, T], F32)
        tbl_sin = sbuf("tbl_sin", [128, T], F32)
        scr = sbuf("scr", [128, 2048], F32)
        sg_sb = sbuf("sg_sb", [128, 64], F32)
        su_sb = sbuf("su_sb", [128, 64], F32)
        sd_sb = sbuf("sd_sb", [128, 16], F32)
        sdh_sb = sbuf("sdh_sb", [128, 16], F32)
        ones_b = sbuf("ones_b", [128, 128], BF16)
        ones_f = sbuf("ones_f", [1, 128], F32)
        eps_sb = sbuf("eps_sb", [1, 1], F32)

        PS = psumt("PS", [128, 2048], F32)
        PS2 = psumt("PS2", [128, 2048], F32)

        sems = {k: ctx.enter_context(nc.semaphore(k))
                for k in ("ls", "lg", "st", "mm", "act", "dve", "pool", "cc")}

        S_ = Sched()
        c = S_.c

        def emit(eng, fn, waits=(), inc=None, amt=1):
            return S_.emit(eng, fn, waits, inc, amt)

        xbf = [wgu_sb[:, 4096 + p * 1024: 4096 + (p + 1) * 1024] for p in (0, 1)]
        xsq = [wgu_sb[:, 6144 + p * 1024: 6144 + (p + 1) * 1024] for p in (0, 1)]
        rowA = scr[0:1, 0:1024]
        rowB = scr[0:1, 1024:2048]
        probsT = wgu_sb

        for dc in range(DC):
            emit("sync", lambda e, dc=dc: e.dma_start(
                out=x_sb[:, dc * T:(dc + 1) * T], in_=xT[dc * 128:(dc + 1) * 128, :]),
                inc="ls", amt=16)
        x_ls = c["ls"]
        emit("sync", lambda e: e.dma_start(out=tbl_cos[:], in_=cos2q_d), inc="ls", amt=16)
        emit("sync", lambda e: e.dma_start(out=tbl_sin[:], in_=sinSq_d), inc="ls", amt=16)
        emit("sync", lambda e: e.dma_start(out=sg_sb[:], in_=sgd), inc="ls", amt=16)
        emit("sync", lambda e: e.dma_start(out=su_sb[:], in_=sud), inc="ls", amt=16)
        emit("sync", lambda e: e.dma_start(out=sd_sb[:], in_=sdd), inc="ls", amt=16)
        misc_ls = c["ls"]
        emit("vector", lambda e: e.memset(ones_b[:], 1.0))
        emit("vector", lambda e: e.memset(ones_f[:], 1.0))
        emit("vector", lambda e: e.memset(eps_sb[:], 1e-5), inc="dve")
        emit("scalar", lambda e: e.activation(sdh_sb[:], sd_sb[:], AF.Copy, scale=0.5),
             waits=[("ls", misc_ls)], inc="act")
        emit("gpsimd", lambda e: e.dma_start(out=wq_bf, in_=wqt), inc="lg", amt=16)
        emit("gpsimd", lambda e: e.dma_start(out=wo_bf, in_=wot), inc="lg", amt=16)
        precast_lg = c["lg"]

        def emit_ln(x_ready, ps_waits, ps2_waits, pool_extra):
            stats_mm = []
            for dc in range(DC):
                par = dc % 2
                war = [("mm", stats_mm[dc - 2])] if dc >= 2 else list(pool_extra)
                emit("gpsimd", lambda e, dc=dc, par=par: e.tensor_copy(
                    xbf[par], x_sb[:, dc * T:(dc + 1) * T]),
                    waits=x_ready(dc) + war, inc="pool")
                pool_v = c["pool"]
                emit("scalar", lambda e, dc=dc, par=par: e.activation(
                    xsq[par], x_sb[:, dc * T:(dc + 1) * T], AF.Square),
                    waits=x_ready(dc) + war, inc="act")
                act_v = c["act"]
                w = [("pool", pool_v), ("act", act_v)] + (list(ps_waits) if dc == 0 else [])

                def mmfn(e, dc=dc, par=par):
                    st, sp = dc == 0, dc == DC - 1
                    i = None
                    for h in (0, 1):
                        e.matmul(PS[0:1, h * 512:(h + 1) * 512], ones_b[:, 0:1],
                                 xbf[par][:, h * 512:(h + 1) * 512], start=st, stop=sp)
                        i = e.matmul(PS[0:1, 1024 + h * 512:1024 + (h + 1) * 512], ones_b[:, 0:1],
                                     xsq[par][:, h * 512:(h + 1) * 512], start=st, stop=sp)
                    return i
                emit("tensor", mmfn, waits=w, inc="mm")
                stats_mm.append(c["mm"])
            emit("scalar", lambda e: e.activation(rowA, PS[0:1, 0:1024], AF.Copy, scale=-1.0 / D),
                 waits=[("mm", stats_mm[-1])])
            emit("scalar", lambda e: e.activation(rowB, PS[0:1, 1024:2048], AF.Copy, scale=1.0 / D),
                 inc="act")
            act_rows = c["act"]

            def bc1(e):
                e.matmul(PS2[:, 0:512], ones_f[:], rowA[0:1, 0:512], start=True, stop=True)
                return e.matmul(PS2[:, 512:1024], ones_f[:], rowA[0:1, 512:1024], start=True, stop=True)
            emit("tensor", bc1, waits=[("act", act_rows)] + list(ps2_waits), inc="mm")
            mm_bc1 = c["mm"]
            emit("vector", lambda e: e.tensor_tensor(rowA, rowA, rowA, ALU.mult),
                 waits=[("mm", mm_bc1), ("act", act_rows)])
            emit("vector", lambda e: e.tensor_tensor(rowB, rowB, rowA, ALU.subtract), inc="dve")
            var_dve = c["dve"]
            emit("scalar", lambda e: e.activation(rowA, rowB, AF.Ln, bias=eps_sb[0:1, 0:1]),
                 waits=[("dve", var_dve)])
            emit("scalar", lambda e: e.activation(rowB, rowA, AF.Exp, scale=-0.5), inc="act")
            act_rstd = c["act"]

            def bc2(e):
                e.matmul(PS2[:, 1024:1536], ones_f[:], rowB[0:1, 0:512], start=True, stop=True)
                return e.matmul(PS2[:, 1536:2048], ones_f[:], rowB[0:1, 512:1024], start=True, stop=True)
            emit("tensor", bc2, waits=[("act", act_rstd)], inc="mm")
            mm_bc2 = c["mm"]
            apply_chunks = []
            for dc in range(DC):
                w = [("mm", mm_bc2)] if dc == 0 else []
                emit("vector", lambda e, dc=dc: e.tensor_tensor(
                    scr[:, 0:1024], x_sb[:, dc * T:(dc + 1) * T], PS2[:, 0:1024], ALU.add), waits=w)
                emit("vector", lambda e, dc=dc: e.tensor_tensor(
                    ln_sb[:, dc * T:(dc + 1) * T], scr[:, 0:1024], PS2[:, 1024:2048], ALU.mult),
                    inc="dve")
                apply_chunks.append(c["dve"])
            return c["dve"], stats_mm[-1], apply_chunks

        def emit_ffn(apply_done, apply_chunks, C_prev_reader_mm, gu0_waits, wgu_load0_waits,
                     wgu_load1_waits, wd_load01_waits):
            gu_mm, silu_act, stt_dve = {}, {}, {}
            qmm, devac = {}, {}
            load_lg, qload_lg = {}, {}
            down_last_mm = {}
            NITER = 128

            def jc_of(n):
                it, j = divmod(n, 32)
                return (it // 2) * 32 + j

            def sched_wgu_load(n, extra):
                par = n % 2
                jcg = jc_of(n)
                emit("gpsimd", lambda e, jcg=jcg, par=par: e.dma_start(
                    out=wgu_sb[:, par * 4096:(par + 1) * 4096], in_=wgu[jcg]),
                    waits=extra, inc="lg", amt=16)
                load_lg[n] = c["lg"]

            sched_wgu_load(0, list(wgu_load0_waits))
            sched_wgu_load(1, list(wgu_load1_waits))

            for it in range(4):
                ihalf, thalf = divmod(it, 2)
                for j in range(32):
                    gidx = it * 32 + j
                    par = gidx % 2
                    jcg = ihalf * 32 + j
                    if gidx + 1 < NITER and (gidx + 1) not in load_lg:
                        sched_wgu_load(gidx + 1, [("mm", gu_mm[gidx - 1])] if gidx >= 1 else [])
                    w = [("lg", load_lg[gidx])]
                    if gidx >= 2:
                        w += [("act", silu_act[gidx - 2]), ("dve", stt_dve[gidx - 2]),
                              ("dve", apply_done)]
                    elif gu0_waits:
                        w += list(gu0_waits)

                    def gu_part(e, par=par, thalf=thalf, which=0, q4=0):
                        base = par * 4096 + (0 if which == 0 else 2048)
                        ups = PS[:, par * 1024 + which * 512:par * 1024 + (which + 1) * 512]
                        i = None
                        for kc in range(q4 * 4, q4 * 4 + 4):
                            i = e.matmul(ups, wgu_sb[:, base + kc * 128:base + (kc + 1) * 128],
                                         ln_sb[:, kc * T + thalf * TH:kc * T + thalf * TH + TH],
                                         start=(kc == 0), stop=(kc == DC - 1))
                        return i
                    if gidx < 2:
                        for which in (0, 1):
                            for q4 in range(4):
                                ww = list(w) + [("dve", apply_chunks[q4 * 4 + 3])]
                                last = which == 1 and q4 == 3
                                emit("tensor", lambda e, par=par, thalf=thalf, which=which, q4=q4:
                                     gu_part(e, par, thalf, which, q4),
                                     waits=ww, inc=("mm" if last else None))
                    else:
                        def gufn(e, par=par, thalf=thalf):
                            i = None
                            for which in (0, 1):
                                for q4 in range(4):
                                    i = gu_part(e, par, thalf, which, q4)
                            return i
                        emit("tensor", gufn, waits=w, inc="mm")
                    gu_mm[gidx] = c["mm"]
                    emit("scalar", lambda e, par=par, jcg=jcg: e.activation(
                        scr[:, par * 512:(par + 1) * 512], PS[:, par * 1024:par * 1024 + 512],
                        AF.Silu, scale=sg_sb[:, jcg:jcg + 1]),
                        waits=[("mm", gu_mm[gidx])], inc="act")
                    silu_act[gidx] = c["act"]
                    w = [("act", silu_act[gidx])]
                    if j == 0:
                        w += [("mm", C_prev_reader_mm if it == 0 else down_last_mm[it - 1])]
                    emit("vector", lambda e, par=par, jcg=jcg, j=j: e.scalar_tensor_tensor(
                        C[:, j * 512:(j + 1) * 512], PS[:, par * 1024 + 512:par * 1024 + 1024],
                        su_sb[:, jcg:jcg + 1], scr[:, par * 512:(par + 1) * 512],
                        ALU.mult, ALU.mult), waits=w, inc="dve")
                    stt_dve[gidx] = c["dve"]
                def sched_wd_load(qg, extra):
                    itq, rem = divmod(qg, 32)
                    mcq, qq = divmod(rem, 2)
                    ihq = itq // 2
                    parq = qg % 2
                    emit("gpsimd", lambda e, mcq=mcq, ihq=ihq, qq=qq, parq=parq: e.dma_start(
                        out=wdw[:, parq * 2048:(parq + 1) * 2048],
                        in_=wdt[mcq, :, ihq * 4096 + qq * 2048:ihq * 4096 + (qq + 1) * 2048]),
                        waits=extra, inc="lg", amt=16)
                    qload_lg[qg] = c["lg"]

                if it == 0:
                    sched_wd_load(0, list(wd_load01_waits))
                    sched_wd_load(1, list(wd_load01_waits))
                for mc in range(16):
                    dit = it * 16 + mc
                    dpar = mc % 2
                    for q in (0, 1):
                        qg = it * 32 + mc * 2 + q
                        if qg + 1 < (it + 1) * 32 + 2 and (qg + 1) not in qload_lg and qg + 1 < 128:
                            sched_wd_load(qg + 1, [("mm", qmm[qg - 1])] if qg >= 1 else [])
                        w = [("lg", qload_lg[qg]), ("dve", stt_dve[it * 32 + 31])]
                        if q == 0 and dit >= 2:
                            w += [("dve", devac[dit - 2])]

                        def dnfn(e, par2=qg % 2, q=q, dpar=dpar):
                            r = None
                            for kcl in range(16):
                                r = e.matmul(PS2[:, dpar * 512:(dpar + 1) * 512],
                                             wdw[:, par2 * 2048 + kcl * 128:par2 * 2048 + (kcl + 1) * 128],
                                             C[:, (q * 16 + kcl) * 512:(q * 16 + kcl + 1) * 512],
                                             start=(q == 0 and kcl == 0), stop=(q == 1 and kcl == 15))
                            return r
                        emit("tensor", dnfn, waits=w, inc="mm")
                        qmm[qg] = c["mm"]
                    emit("vector", lambda e, mc=mc, thalf=thalf, dpar=dpar: e.scalar_tensor_tensor(
                        x_sb[:, mc * T + thalf * TH:mc * T + thalf * TH + TH],
                        PS2[:, dpar * 512:(dpar + 1) * 512], sdh_sb[:, mc:mc + 1],
                        x_sb[:, mc * T + thalf * TH:mc * T + thalf * TH + TH],
                        ALU.mult, ALU.add), waits=[("mm", qmm[it * 32 + mc * 2 + 1])], inc="dve")
                    devac[dit] = c["dve"]
                down_last_mm[it] = qmm[it * 32 + 31]
            return dict(gu_last_mm=gu_mm[127], silu_last=silu_act[127], stt_last=stt_dve[127],
                        down_last_mm=down_last_mm[3], devac_last=devac[63], devac=devac)

        def emit_store(chunk_dve):
            for dc in range(DC):
                w = [("dve", chunk_dve(dc))]
                emit("sync", lambda e, dc=dc: e.dma_start(
                    out=yT[dc * 128:(dc + 1) * 128, :], in_=x_sb[:, dc * T:(dc + 1) * T]),
                    waits=w, inc="st", amt=16)
            emit("sync", lambda e: e.nop(), waits=[("st", c["st"])])

        def finish():
            block = ctx.enter_context(nc.Block())

            def mk(name):
                def body(e):
                    for waits, fn, trip in S_.ops[name]:
                        for (s, v) in waits:
                            e.wait_ge(sems[s], v)
                        inst = fn(e)
                        if trip is not None:
                            if inst is None:
                                inst = e.nop()
                            sname, amt = trip
                            inst.then_inc(sems[sname], amt)
                return body
            for name in ENGINES:
                getattr(block, name)(mk(name))
            return nc

        apply1, ln1_stats, applyc1 = emit_ln(
            x_ready=lambda dc: [("ls", 16 * (dc + 1))], ps_waits=[], ps2_waits=[], pool_extra=[])
        f1 = emit_ffn(apply_done=apply1, apply_chunks=applyc1, C_prev_reader_mm=0, gu0_waits=[],
                      wgu_load0_waits=[], wgu_load1_waits=[("mm", ln1_stats)],
                      wd_load01_waits=[])
        if upto == "ffn1":
            emit_store(lambda dc: f1["devac"][48 + dc])
            return finish()

        apply2, ln2_stats, applyc2 = emit_ln(
            x_ready=lambda dc: [("dve", f1["devac"][48 + dc])],
            ps_waits=[("act", f1["silu_last"]), ("dve", f1["stt_last"])],
            ps2_waits=[("dve", f1["devac_last"])],
            pool_extra=[("mm", f1["gu_last_mm"])])

        dbg_gate = []
        if dbg:
            emit("sync", lambda e: e.dma_start(out=dbg_ln2, in_=ln_sb[:]),
                 waits=[("dve", apply2)], inc="st", amt=16)
            dbg_gate.append(("st", c["st"]))

        qproj_mm, qrope_dve, wq_lg = [], [], []
        for oc in range(NH):
            par = oc % 2
            w = [("mm", f1["down_last_mm"])] if oc < 2 else [("mm", qproj_mm[oc - 2])]
            if oc == 0:
                w = w + [("lg", precast_lg)]
            emit("sync", lambda e, oc=oc, par=par: e.dma_start(
                out=wdw[:, par * 2048:(par + 1) * 2048], in_=wq_bf[oc]), waits=w, inc="ls", amt=16)
            wq_lg.append(c["ls"])
            w = [("ls", wq_lg[oc])]
            w += [("dve", qrope_dve[oc - 2])] if oc >= 2 else [("dve", f1["devac_last"])]
            if oc < 2:
                w += [("act", f1["silu_last"]), ("dve", f1["stt_last"])]
            else:
                w += [("dve", apply2)]

            def qp_part(e, par=par, nh=0, q4=0):
                r = None
                for kc in range(q4 * 4, q4 * 4 + 4):
                    r = e.matmul(PS[:, par * 1024 + nh * 512:par * 1024 + (nh + 1) * 512],
                                 wdw[:, par * 2048 + kc * 128:par * 2048 + (kc + 1) * 128],
                                 ln_sb[:, kc * T + nh * 512:kc * T + (nh + 1) * 512],
                                 start=(kc == 0), stop=(kc == DC - 1))
                return r
            if oc < 2:
                for nh in (0, 1):
                    for q4 in range(4):
                        ww = list(w) + [("dve", applyc2[q4 * 4 + 3])]
                        last = nh == 1 and q4 == 3
                        emit("tensor", lambda e, par=par, nh=nh, q4=q4: qp_part(e, par, nh, q4),
                             waits=ww, inc=("mm" if last else None))
            else:
                def projfn2(e, par=par):
                    r = None
                    for nh in (0, 1):
                        for q4 in range(4):
                            r = qp_part(e, par, nh, q4)
                    return r
                emit("tensor", projfn2, waits=w, inc="mm")
            qproj_mm.append(c["mm"])
            w = [("mm", qproj_mm[oc]), ("ls", misc_ls)]

            def qropefn(e, par=par, oc=oc):
                ps_ = PS[:, par * 1024:(par + 1) * 1024]
                e.tensor_tensor(scr[:, 0:1024], ps_, tbl_cos[:], ALU.mult)
                e.tensor_tensor(scr[0:64, 1024:2048], ps_[64:128, :], tbl_sin[0:64, :], ALU.mult)
                e.tensor_tensor(scr[64:128, 1024:2048], ps_[0:64, :], tbl_sin[64:128, :], ALU.mult)
                return e.tensor_tensor(C[:, oc * 1024:(oc + 1) * 1024],
                                       scr[:, 0:1024], scr[:, 1024:2048], ALU.add)
            emit("vector", qropefn, waits=w, inc="dve")
            qrope_dve.append(c["dve"])

        emit("sync", lambda e: e.dma_start(out=tbl_cos[:], in_=cos2k_d),
             waits=[("dve", qrope_dve[-1])], inc="ls", amt=16)
        emit("sync", lambda e: e.dma_start(out=tbl_sin[:], in_=sinSk_d), inc="ls", amt=16)
        tblk_ls = c["ls"]
        kproj_mm, krope_dve, wk_lg = [], [], []
        for oc in range(NKV):
            par = oc % 2
            w = [("mm", qproj_mm[-1])] if oc < 2 else [("mm", kproj_mm[oc - 2])]
            emit("gpsimd", lambda e, oc=oc, par=par: e.dma_start(
                out=wdw[:, par * 2048:(par + 1) * 2048], in_=wkt[oc]), waits=w, inc="lg", amt=16)
            wk_lg.append(c["lg"])
            w = [("lg", wk_lg[oc])]
            w += [("dve", krope_dve[oc - 2])] if oc >= 2 else [("dve", qrope_dve[-1])]

            def projfn(e, par=par):
                r = None
                for nh in (0, 1):
                    for kc in range(DC):
                        r = e.matmul(PS[:, par * 1024 + nh * 512:par * 1024 + (nh + 1) * 512],
                                     wdw[:, par * 2048 + kc * 128:par * 2048 + (kc + 1) * 128],
                                     ln_sb[:, kc * T + nh * 512:kc * T + (nh + 1) * 512],
                                     start=(kc == 0), stop=(kc == DC - 1))
                return r
            emit("tensor", projfn, waits=w, inc="mm")
            kproj_mm.append(c["mm"])
            w = [("mm", kproj_mm[oc])]
            if oc == 0:
                w += [("ls", tblk_ls)]

            def ropefn(e, par=par, oc=oc):
                ps_ = PS[:, par * 1024:(par + 1) * 1024]
                e.tensor_tensor(scr[:, 0:1024], ps_, tbl_cos[:], ALU.mult)
                e.tensor_tensor(scr[0:64, 1024:2048], ps_[64:128, :], tbl_sin[0:64, :], ALU.mult)
                e.tensor_tensor(scr[64:128, 1024:2048], ps_[0:64, :], tbl_sin[64:128, :], ALU.mult)
                return e.tensor_tensor(kvK[:, oc * 1024:(oc + 1) * 1024],
                                       scr[:, 0:1024], scr[:, 1024:2048], ALU.add)
            emit("vector", ropefn, waits=w, inc="dve")
            krope_dve.append(c["dve"])

        vproj_mm, vevac_act = [], []
        for dhalf in (0, 1):
            wv_w = [("mm", kproj_mm[-1])] if dhalf == 0 else [("mm", vproj_mm[-1])]
            emit("gpsimd", lambda e, dhalf=dhalf: e.dma_start(
                out=wdw[:].rearrange("p (k o) -> p k o", o=256),
                in_=wvr[:, :, dhalf * 256:(dhalf + 1) * 256].rearrange("k p o -> p k o")),
                waits=wv_w, inc="lg", amt=16)
            wv_lg = c["lg"]
            for tc in range(8):
                vit = dhalf * 8 + tc
                par = vit % 2
                w = [("lg", wv_lg), ("dve", apply2)]
                if vit >= 2:
                    w += [("act", vevac_act[vit - 2])]
                else:
                    w += [("dve", f1["devac_last"])]

                def vpfn(e, par=par, tc=tc):
                    r = None
                    for kc in range(DC):
                        r = e.matmul(PS2[:, par * 512:par * 512 + 256],
                                     ln_sb[:, kc * T + tc * 128:kc * T + (tc + 1) * 128],
                                     wdw[:, kc * 256:(kc + 1) * 256],
                                     start=(kc == 0), stop=(kc == DC - 1))
                    return r
                emit("tensor", vpfn, waits=w, inc="mm")
                vproj_mm.append(c["mm"])
                emit("scalar", lambda e, par=par, tc=tc, dhalf=dhalf: e.activation(
                    kvK[:, 4096 + tc * 512 + dhalf * 256:4096 + tc * 512 + dhalf * 256 + 256],
                    PS2[:, par * 512:par * 512 + 256], AF.Copy),
                    waits=[("mm", vproj_mm[-1])], inc="act")
                vevac_act.append(c["act"])

        stage_w = [("act", vevac_act[-1]), ("dve", krope_dve[-1])]
        emit("gpsimd", lambda e: e.dma_start(
            out=kv_in[0:512, :].rearrange("(cc p) t -> p cc t", p=128),
            in_=kvK[:, 0:4096].rearrange("p (cc t) -> p cc t", t=1024)),
            waits=stage_w, inc="lg", amt=16)
        vbytes = kv_in[512:1024, :].rearrange("r2 (h d) -> (r2 h) d", d=512)
        emit("gpsimd", lambda e: e.dma_start(
            out=vbytes.rearrange("(cc p) d -> p cc d", p=128),
            in_=kvK[:, 4096:8192].rearrange("p (cc d) -> p cc d", d=512)),
            inc="lg", amt=16)
        kvout_lg = c["lg"]
        emit("gpsimd", lambda e: e.collective_compute(
            "AllGather", ALU.bypass, replica_groups=[list(range(NCORES))],
            ins=[kv_in.opt()], outs=[kv_ag.opt()]),
            waits=[("lg", kvout_lg)], inc="cc")
        cc_done = c["cc"]
        if dbg:
            emit("gpsimd", lambda e: e.dma_start(out=dbg_kvag, in_=kv_ag),
                 waits=[("cc", cc_done)], inc="lg", amt=16)
            dbg_gate.append(("lg", c["lg"]))
            emit("sync", lambda e: e.dma_start(out=dbg_q, in_=C[:]),
                 waits=[("dve", qrope_dve[-1])], inc="st", amt=16)
            dbg_gate.append(("st", c["st"]))
            emit("sync", lambda e: e.dma_start(out=dbg_kvst, in_=kvK[:]),
                 waits=list(stage_w), inc="st", amt=16)
            dbg_gate.append(("st", c["st"]))

        probsT = wgu_sb
        pq1 = wgu_sb[:, 4096:4608]
        pq2 = wgu_sb[:, 4608:5120]
        qq = [wgu_sb[:, 5120:5632], wgu_sb[:, 5632:6144]]
        oo = [wgu_sb[:, 6144:6656], wgu_sb[:, 6656:7168]]
        exp_act, scpair_mm, pv_mm, oevac_dve = [], [], [], []
        padd_dve, qadd_dve, oadd_dve, rs_mm = [], [], [], []
        last_tile_mm = []
        last_sc_of_g, last_pv_of_g = {}, {}
        k_ls, v_ls = [], []
        gpair = 0
        for g in range(NKV):
            w = [("cc", cc_done), ("lg", kvout_lg)]
            if g >= 1:
                w += [("mm", last_sc_of_g[g - 1])]
            emit("sync", lambda e, g=g: e.dma_start(
                out=kvK[:].rearrange("p (r t) -> p r t", t=1024),
                in_=kv_ag.rearrange("(r x) t -> r x t", x=1024)[:, g * 128:(g + 1) * 128, :].rearrange("r p t -> p r t")),
                waits=w, inc="ls", amt=16)
            k_ls.append(c["ls"])
            w2 = [("mm", last_pv_of_g[g - 1])] if g >= 1 else []
            for r in range(8):
                vsrc = kv_ag[r * 1024 + 512:(r + 1) * 1024, :].rearrange(
                    "r2 (h d) -> (r2 h) d", d=512)[:, g * 128:(g + 1) * 128]
                emit("sync", lambda e, r=r, vsrc=vsrc: e.dma_start(
                    out=vg[:, r * 1024:(r + 1) * 1024].rearrange("p (s d) -> p s d", d=128),
                    in_=vsrc.rearrange("(s p) d -> p s d", p=128)),
                    waits=(w2 if r == 0 else []), inc="ls", amt=16)
            v_ls.append(c["ls"])
            for h in range(4):
                hh = g * 4 + h
                for qcg in (0, 1):
                    tile = hh * 2 + qcg
                    tpar = tile % 2
                    Ops = PS2[:, tpar * 1024:tpar * 1024 + 512]
                    RSps = PS2[:, tpar * 1024 + 512:tpar * 1024 + 1024]
                    tile_base_pair = gpair
                    tile_q_base = len(qadd_dve)
                    tile_o_base = len(oadd_dve)
                    first_tile_of_g = (h == 0 and qcg == 0)

                    def scfn(e, p, hh=hh, qcg=qcg):
                        r = None
                        for k in (0, 1):
                            cch = 2 * p + k
                            slot = cch % 4
                            r = e.matmul(PS[:, slot * 512:(slot + 1) * 512],
                                         kvK[:, cch * 128:(cch + 1) * 128],
                                         C[:, hh * 1024 + qcg * 512:hh * 1024 + qcg * 512 + 512],
                                         start=True, stop=True)
                        return r
                    w = [("ls", k_ls[g])]
                    if gpair >= 2:
                        w += [("act", exp_act[gpair - 2])]
                    else:
                        w += [("dve", krope_dve[-1])]
                    emit("tensor", lambda e, scfn=scfn: scfn(e, 0), waits=w, inc="mm")
                    scpair_mm.append(c["mm"])
                    gpair += 1
                    for p in range(32):
                        gp = tile_base_pair + p
                        w = [("mm", scpair_mm[gp])]
                        if gp >= 4:
                            w += [("mm", pv_mm[gp - 4]), ("dve", padd_dve[gp - 4])]

                        def expfn(e, p=p, gp=gp):
                            return e.activation(probsT[:, (gp % 4) * 1024:(gp % 4 + 1) * 1024],
                                                PS[:, (p % 2) * 1024:(p % 2 + 1) * 1024], AF.Exp)
                        emit("scalar", expfn, waits=w, inc="act")
                        exp_act.append(c["act"])
                        def paddfn(e, p=p, gp=gp):
                            half = gp % 4
                            dst = pq1 if p % 2 == 0 else pq2
                            return e.tensor_tensor(dst, probsT[:, half * 1024:half * 1024 + 512],
                                                   probsT[:, half * 1024 + 512:(half + 1) * 1024],
                                                   ALU.add)
                        emit("vector", paddfn, waits=[("act", exp_act[gp])], inc="dve")
                        padd_dve.append(c["dve"])
                        if p % 2 == 1:
                            emit("vector", lambda e, j=(p // 2) % 2: e.tensor_tensor(
                                qq[j], pq1, pq2, ALU.add), inc="dve")
                            qadd_dve.append(c["dve"])
                        if p % 4 == 3:
                            j8 = p // 4
                            o_glob = len(oadd_dve)
                            wo_ = [("mm", rs_mm[o_glob - 2])] if o_glob >= 2 else []
                            emit("vector", lambda e, j8=j8: e.tensor_tensor(
                                oo[j8 % 2], qq[0], qq[1], ALU.add), waits=wo_, inc="dve")
                            oadd_dve.append(c["dve"])
                        if p + 1 < 32:
                            w = []
                            if gpair >= 2:
                                w += [("act", exp_act[gpair - 2])]
                            else:
                                w += [("dve", krope_dve[-1])]
                            emit("tensor", lambda e, scfn=scfn, p=p: scfn(e, p + 1), waits=w, inc="mm")
                            scpair_mm.append(c["mm"])
                            gpair += 1
                        w = [("act", exp_act[gp])]
                        if p == 0:
                            if tile >= 2:
                                w += [("dve", oevac_dve[tile - 2])]
                            else:
                                w += [("act", vevac_act[-1])]
                            if first_tile_of_g:
                                w += [("ls", v_ls[g])]

                        def pvfn(e, p=p, gp=gp, Ops=Ops):
                            r = None
                            for k in (0, 1):
                                cch = 2 * p + k
                                slot = 2 * (gp % 4) + k
                                r = e.matmul(Ops, vg[:, cch * 128:(cch + 1) * 128],
                                             probsT[:, slot * 512:(slot + 1) * 512],
                                             start=(cch == 0), stop=(cch == 63))
                            return r
                        emit("tensor", pvfn, waits=w, inc="mm")
                        pv_mm.append(c["mm"])
                        if (p % 4 == 0 and p >= 4) or p == 31:
                            for j8 in ([(p - 4) // 4] if p != 31 else [7]):
                                emit("tensor", lambda e, j8=j8, RSps=RSps: e.matmul(
                                    RSps, ones_b[:], oo[j8 % 2],
                                    start=(j8 == 0), stop=(j8 == 7)),
                                    waits=[("dve", oadd_dve[tile_o_base + j8])], inc="mm")
                                rs_mm.append(c["mm"])
                    last_sc_of_g[g] = scpair_mm[-1]
                    last_pv_of_g[g] = pv_mm[-1]
                    last_tile_mm.append(rs_mm[-1])

                    def oefn(e, Ops=Ops, RSps=RSps, tpar=tpar, hh=hh, qcg=qcg):
                        e.reciprocal(scr[:, tpar * 512:(tpar + 1) * 512], RSps)
                        return e.tensor_tensor(
                            ln_sb[:, hh * 1024 + qcg * 512:hh * 1024 + qcg * 512 + 512],
                            Ops, scr[:, tpar * 512:(tpar + 1) * 512], ALU.mult)
                    wv_ = [("mm", rs_mm[-1])] + (dbg_gate if (dbg and tile == 0) else [])
                    emit("vector", oefn, waits=wv_, inc="dve")
                    oevac_dve.append(c["dve"])

        wo_mm, woevac_dve, wo_lg = [], [], []
        for oc in range(NH):
            par = oc % 2
            w = [("mm", vproj_mm[-1])] if oc < 2 else [("mm", wo_mm[2 * (oc - 2) + 1])]
            emit("sync", lambda e, oc=oc, par=par: e.dma_start(
                out=wdw[:, par * 2048:(par + 1) * 2048], in_=wo_bf[oc]), waits=w, inc="ls", amt=16)
            wo_lg.append(c["ls"])
            for th in (0, 1):
                woit = oc * 2 + th
                wpar = woit % 4
                w = [("ls", wo_lg[oc]), ("dve", oevac_dve[-1])]
                if woit < 4:
                    w += [("act", exp_act[-1])]
                else:
                    w += [("dve", woevac_dve[woit - 4])]

                def wofn(e, par=par, th=th, wpar=wpar):
                    r = None
                    for hc in range(DC):
                        r = e.matmul(PS[:, wpar * 512:(wpar + 1) * 512],
                                     wdw[:, par * 2048 + hc * 128:par * 2048 + (hc + 1) * 128],
                                     ln_sb[:, hc * T + th * TH:hc * T + th * TH + TH],
                                     start=(hc == 0), stop=(hc == DC - 1))
                    return r
                emit("tensor", wofn, waits=w, inc="mm")
                wo_mm.append(c["mm"])
                emit("vector", lambda e, oc=oc, th=th, wpar=wpar: e.tensor_tensor(
                    x_sb[:, oc * T + th * TH:oc * T + th * TH + TH],
                    x_sb[:, oc * T + th * TH:oc * T + th * TH + TH],
                    PS[:, wpar * 512:(wpar + 1) * 512], ALU.add),
                    waits=[("mm", wo_mm[-1])], inc="dve")
                woevac_dve.append(c["dve"])

        if upto == "attn":
            emit_store(lambda dc: woevac_dve[2 * dc + 1])
            return finish()

        apply3, ln3_stats, applyc3 = emit_ln(
            x_ready=lambda dc: [("dve", woevac_dve[2 * dc + 1])],
            ps_waits=[("dve", woevac_dve[-1])],
            ps2_waits=[("dve", oevac_dve[-1])],
            pool_extra=[("mm", ln2_stats)])
        f2 = emit_ffn(apply_done=apply3, apply_chunks=applyc3, C_prev_reader_mm=scpair_mm[-1],
                      gu0_waits=[("dve", woevac_dve[-1]), ("mm", last_tile_mm[-1])],
                      wgu_load0_waits=[("mm", last_tile_mm[-1])],
                      wgu_load1_waits=[("mm", ln3_stats)],
                      wd_load01_waits=[("mm", wo_mm[-1])])

        emit_store(lambda dc: f2["devac"][48 + dc])
        return finish()



def _prep_weights(Wq, Wk, Wv, Wo, Wg, Wu, Wd):
    wgu_h = np.concatenate([
        np.ascontiguousarray(Wg.reshape(16, 128, 64, 128).transpose(2, 1, 0, 3)).reshape(64, 128, 2048),
        np.ascontiguousarray(Wu.reshape(16, 128, 64, 128).transpose(2, 1, 0, 3)).reshape(64, 128, 2048),
    ], axis=2)
    wgu_h = np.ascontiguousarray(wgu_h)
    wdt_h = np.ascontiguousarray(
        Wd.reshape(64, 128, 16, 128).transpose(2, 1, 0, 3)).reshape(16, 128, 8192).copy()
    wqt_h = np.ascontiguousarray(
        Wq.reshape(16, 128, 16, 128).transpose(0, 3, 2, 1)).reshape(16, 128, 2048).copy()
    wkt_h = np.ascontiguousarray(
        Wk.reshape(4, 128, 16, 128).transpose(0, 3, 2, 1)).reshape(4, 128, 2048).copy()
    wvr_h = np.ascontiguousarray(Wv.T).reshape(16, 128, 512).copy()
    wot_h = np.ascontiguousarray(
        Wo.reshape(16, 128, 16, 128).transpose(0, 3, 2, 1)).reshape(16, 128, 2048).copy()
    return wgu_h, wdt_h, wqt_h, wkt_h, wvr_h, wot_h


def _rope_tables(core):
    pos = (np.arange(T, dtype=np.float32) + np.float32(core * T))
    inv = (1.0 / (np.float32(10000.0) ** (np.arange(0, HD, 2, dtype=np.float32)
                                          / np.float32(HD)))).astype(np.float32)
    freqs = pos[:, None] * inv[None, :]
    cos = np.cos(freqs).astype(np.float32)
    sin = np.sin(freqs).astype(np.float32)
    cos2 = np.ascontiguousarray(np.concatenate([cos.T, cos.T], axis=0))
    sinS = np.ascontiguousarray(np.concatenate([-sin.T, sin.T], axis=0))
    sc = np.float32(1.0 / math.sqrt(HD))
    return cos2, sinS, np.ascontiguousarray(cos2 * sc), np.ascontiguousarray(sinS * sc)


def run_device(inputs, upto="full"):
    x = np.asarray(inputs["L0_attn_hidden_states"], dtype=np.float32)
    wgu_h, wdt_h, wqt_h, wkt_h, wvr_h, wot_h = _prep_weights(
        np.asarray(inputs["L0_attn_Wq"], np.float32), np.asarray(inputs["L0_attn_Wk"], np.float32),
        np.asarray(inputs["L0_attn_Wv"], np.float32), np.asarray(inputs["L0_attn_Wo"], np.float32),
        np.asarray(inputs["L0_ffn_w_gate"], np.int8), np.asarray(inputs["L0_ffn_w_up"], np.int8),
        np.asarray(inputs["L0_ffn_w_down"], np.int8))
    sgl = np.ascontiguousarray(np.asarray(inputs["L0_ffn_s_gate"], np.float32).reshape(64, 128).T)
    sul = np.ascontiguousarray(np.asarray(inputs["L0_ffn_s_up"], np.float32).reshape(64, 128).T)
    sdl = np.ascontiguousarray(np.asarray(inputs["L0_ffn_s_down"], np.float32).reshape(16, 128).T)

    xTfull = np.ascontiguousarray(x[0].T)
    nc = build_program(upto)
    in_maps = []
    for i in range(NCORES):
        cos2k, sinSk, cos2q, sinSq = _rope_tables(i)
        in_maps.append({
            "xT": np.ascontiguousarray(xTfull[:, i * T:(i + 1) * T]),
            "wgu": wgu_h, "wdt": wdt_h, "wqt": wqt_h, "wkt": wkt_h, "wvr": wvr_h, "wot": wot_h,
            "sgd": sgl, "sud": sul, "sdd": sdl,
            "cos2k": cos2k, "sinSk": sinSk, "cos2q": cos2q, "sinSq": sinSq,
        })
    res = run_bass_kernel_spmd(nc, in_maps, core_ids=list(range(NCORES)))
    outT = np.concatenate([np.asarray(r["yT"]) for r in res.results], axis=1)
    return np.ascontiguousarray(outT.T)[None].astype(np.float32)


def kernel(**inputs):
    return run_device(inputs, "full")


# version 09
# speedup vs baseline: 1.0188x; 1.0188x over previous
import math
import numpy as np
import concourse.bass as bass
import concourse.mybir as mybir
from concourse.bass_utils import run_bass_kernel_spmd
from contextlib import ExitStack

F32 = mybir.dt.float32
BF16 = mybir.dt.bfloat16
I8 = mybir.dt.int8
AF = mybir.ActivationFunctionType
ALU = mybir.AluOpType

NCORES = 8
T = 1024
TH = 512
D = 2048
DC = 16
HD = 128
NH = 16
NKV = 4

ENGINES = ("sync", "gpsimd", "tensor", "vector", "scalar")
ATTN_SAFE = False
NOP_SPACER = False
TILE_SERIAL = False
DELAY_KV0 = False
QSER = True
QSER2 = False
DELAY_CC = False


class Sched:

    def __init__(self):
        self.ops = {e: [] for e in ENGINES}
        self.c = {k: 0 for k in ("ls", "lg", "st", "mm", "act", "dve", "pool", "cc")}

    def emit(self, eng, fn, waits=(), inc=None, amt=1):
        waits = [(s, v) for (s, v) in waits if v > 0]
        trip = None
        if inc is not None:
            self.c[inc] += amt
            trip = (inc, amt)
        self.ops[eng].append((waits, fn, trip))
        return self.c[inc] if inc is not None else None


def build_program(upto="full", dbg=False):
    nc = bass.Bass("TRN2", target_bir_lowering=False)

    xT = nc.dram_tensor("xT", [D, T], F32, kind="ExternalInput").ap()
    wgu = nc.dram_tensor("wgu", [64, 128, 4096], I8, kind="ExternalInput").ap()
    wdt = nc.dram_tensor("wdt", [16, 128, 8192], I8, kind="ExternalInput").ap()
    wqt = nc.dram_tensor("wqt", [16, 128, 2048], F32, kind="ExternalInput").ap()
    wkt = nc.dram_tensor("wkt", [4, 128, 2048], F32, kind="ExternalInput").ap()
    wvr = nc.dram_tensor("wvr", [16, 128, 512], F32, kind="ExternalInput").ap()
    wot = nc.dram_tensor("wot", [16, 128, 2048], F32, kind="ExternalInput").ap()
    sgd = nc.dram_tensor("sgd", [128, 64], F32, kind="ExternalInput").ap()
    sud = nc.dram_tensor("sud", [128, 64], F32, kind="ExternalInput").ap()
    sdd = nc.dram_tensor("sdd", [128, 16], F32, kind="ExternalInput").ap()
    cos2k_d = nc.dram_tensor("cos2k", [128, T], F32, kind="ExternalInput").ap()
    sinSk_d = nc.dram_tensor("sinSk", [128, T], F32, kind="ExternalInput").ap()
    cos2q_d = nc.dram_tensor("cos2q", [128, T], F32, kind="ExternalInput").ap()
    sinSq_d = nc.dram_tensor("sinSq", [128, T], F32, kind="ExternalInput").ap()

    yT = nc.dram_tensor("yT", [D, T], F32, kind="ExternalOutput").ap()

    if dbg:
        dbg_ln2 = nc.dram_tensor("dbg_ln2", [128, DC * T], BF16, kind="ExternalOutput").ap()
        dbg_q = nc.dram_tensor("dbg_q", [128, 16384], BF16, kind="ExternalOutput").ap()
        dbg_kvst = nc.dram_tensor("dbg_kvst", [128, 8192], BF16, kind="ExternalOutput").ap()
        dbg_kvag = nc.dram_tensor("dbg_kvag", [8192, T], BF16, kind="ExternalOutput").ap()
        dbg_O = nc.dram_tensor("dbg_O", [128, DC * T], BF16, kind="ExternalOutput").ap()
        dbg_kg = nc.dram_tensor("dbg_kg", [4, 128, 8192], BF16, kind="ExternalOutput").ap()
        dbg_q2 = nc.dram_tensor("dbg_q2", [128, 16384], BF16, kind="ExternalOutput").ap()
        dbg_vg = nc.dram_tensor("dbg_vg", [4, 128, 8192], BF16, kind="ExternalOutput").ap()
    wq_bf = nc.dram_tensor("wq_bf", [16, 128, 2048], BF16).ap()
    wo_bf = nc.dram_tensor("wo_bf", [16, 128, 2048], BF16).ap()
    kv_in = nc.dram_tensor("kv_in", [1024, T], BF16).ap()
    kv_ag = nc.dram_tensor("kv_ag", [8 * 1024, T], BF16, addr_space="Shared").ap()

    with ExitStack() as ctx:
        sbuf = lambda n, s, d: ctx.enter_context(nc.sbuf_tensor(n, s, d))
        psumt = lambda n, s, d: ctx.enter_context(nc.psum_tensor(n, s, d))

        x_sb = sbuf("x_sb", [128, DC * T], F32)
        ln_sb = sbuf("ln_sb", [128, DC * T], BF16)
        C = sbuf("C", [128, 16384], BF16)
        kvK = sbuf("kvK", [128, 8192], BF16)
        vg = sbuf("vg", [128, 8192], BF16)
        wgu_sb = sbuf("wgu_sb", [128, 8192], BF16)
        wdw = sbuf("wdw", [128, 4096], BF16)
        tbl_cos = sbuf("tbl_cos", [128, T], F32)
        tbl_sin = sbuf("tbl_sin", [128, T], F32)
        scr = sbuf("scr", [128, 2048], F32)
        sg_sb = sbuf("sg_sb", [128, 64], F32)
        su_sb = sbuf("su_sb", [128, 64], F32)
        sd_sb = sbuf("sd_sb", [128, 16], F32)
        sdh_sb = sbuf("sdh_sb", [128, 16], F32)
        ones_b = sbuf("ones_b", [128, 128], BF16)
        ones_f = sbuf("ones_f", [1, 128], F32)
        eps_sb = sbuf("eps_sb", [1, 1], F32)

        PS = psumt("PS", [128, 2048], F32)
        PS2 = psumt("PS2", [128, 2048], F32)

        sems = {k: ctx.enter_context(nc.semaphore(k))
                for k in ("ls", "lg", "st", "mm", "act", "dve", "pool", "cc")}

        S_ = Sched()
        c = S_.c

        def emit(eng, fn, waits=(), inc=None, amt=1):
            return S_.emit(eng, fn, waits, inc, amt)

        xbf = [wgu_sb[:, 4096 + p * 1024: 4096 + (p + 1) * 1024] for p in (0, 1)]
        xsq = [wgu_sb[:, 6144 + p * 1024: 6144 + (p + 1) * 1024] for p in (0, 1)]
        rowA = scr[0:1, 0:1024]
        rowB = scr[0:1, 1024:2048]
        probsT = wgu_sb

        for dc in range(DC):
            emit("sync", lambda e, dc=dc: e.dma_start(
                out=x_sb[:, dc * T:(dc + 1) * T], in_=xT[dc * 128:(dc + 1) * 128, :]),
                inc="ls", amt=16)
        x_ls = c["ls"]
        emit("sync", lambda e: e.dma_start(out=tbl_cos[:], in_=cos2q_d), inc="ls", amt=16)
        emit("sync", lambda e: e.dma_start(out=tbl_sin[:], in_=sinSq_d), inc="ls", amt=16)
        emit("sync", lambda e: e.dma_start(out=sg_sb[:], in_=sgd), inc="ls", amt=16)
        emit("sync", lambda e: e.dma_start(out=su_sb[:], in_=sud), inc="ls", amt=16)
        emit("sync", lambda e: e.dma_start(out=sd_sb[:], in_=sdd), inc="ls", amt=16)
        misc_ls = c["ls"]
        emit("vector", lambda e: e.memset(ones_b[:], 1.0))
        emit("vector", lambda e: e.memset(ones_f[:], 1.0))
        emit("vector", lambda e: e.memset(eps_sb[:], 1e-5), inc="dve")
        emit("scalar", lambda e: e.activation(sdh_sb[:], sd_sb[:], AF.Copy, scale=0.5),
             waits=[("ls", misc_ls)], inc="act")
        emit("gpsimd", lambda e: e.dma_start(out=wq_bf, in_=wqt), inc="lg", amt=16)
        emit("gpsimd", lambda e: e.dma_start(out=wo_bf, in_=wot), inc="lg", amt=16)
        precast_lg = c["lg"]

        def emit_ln(x_ready, ps_waits, ps2_waits, pool_extra):
            stats_mm = []
            for dc in range(DC):
                par = dc % 2
                war = [("mm", stats_mm[dc - 2])] if dc >= 2 else list(pool_extra)
                emit("gpsimd", lambda e, dc=dc, par=par: e.tensor_copy(
                    xbf[par], x_sb[:, dc * T:(dc + 1) * T]),
                    waits=x_ready(dc) + war, inc="pool")
                pool_v = c["pool"]
                emit("scalar", lambda e, dc=dc, par=par: e.activation(
                    xsq[par], x_sb[:, dc * T:(dc + 1) * T], AF.Square),
                    waits=x_ready(dc) + war, inc="act")
                act_v = c["act"]
                w = [("pool", pool_v), ("act", act_v)] + (list(ps_waits) if dc == 0 else [])

                def mmfn(e, dc=dc, par=par):
                    st, sp = dc == 0, dc == DC - 1
                    i = None
                    for h in (0, 1):
                        e.matmul(PS[0:1, h * 512:(h + 1) * 512], ones_b[:, 0:1],
                                 xbf[par][:, h * 512:(h + 1) * 512], start=st, stop=sp)
                        i = e.matmul(PS[0:1, 1024 + h * 512:1024 + (h + 1) * 512], ones_b[:, 0:1],
                                     xsq[par][:, h * 512:(h + 1) * 512], start=st, stop=sp)
                    return i
                emit("tensor", mmfn, waits=w, inc="mm")
                stats_mm.append(c["mm"])
            emit("scalar", lambda e: e.activation(rowA, PS[0:1, 0:1024], AF.Copy, scale=-1.0 / D),
                 waits=[("mm", stats_mm[-1])])
            emit("scalar", lambda e: e.activation(rowB, PS[0:1, 1024:2048], AF.Copy, scale=1.0 / D),
                 inc="act")
            act_rows = c["act"]

            def bc1(e):
                e.matmul(PS2[:, 0:512], ones_f[:], rowA[0:1, 0:512], start=True, stop=True)
                return e.matmul(PS2[:, 512:1024], ones_f[:], rowA[0:1, 512:1024], start=True, stop=True)
            emit("tensor", bc1, waits=[("act", act_rows)] + list(ps2_waits), inc="mm")
            mm_bc1 = c["mm"]
            emit("vector", lambda e: e.tensor_tensor(rowA, rowA, rowA, ALU.mult),
                 waits=[("mm", mm_bc1), ("act", act_rows)])
            emit("vector", lambda e: e.tensor_tensor(rowB, rowB, rowA, ALU.subtract), inc="dve")
            var_dve = c["dve"]
            emit("scalar", lambda e: e.activation(rowA, rowB, AF.Ln, bias=eps_sb[0:1, 0:1]),
                 waits=[("dve", var_dve)])
            emit("scalar", lambda e: e.activation(rowB, rowA, AF.Exp, scale=-0.5), inc="act")
            act_rstd = c["act"]

            def bc2(e):
                e.matmul(PS2[:, 1024:1536], ones_f[:], rowB[0:1, 0:512], start=True, stop=True)
                return e.matmul(PS2[:, 1536:2048], ones_f[:], rowB[0:1, 512:1024], start=True, stop=True)
            emit("tensor", bc2, waits=[("act", act_rstd)], inc="mm")
            mm_bc2 = c["mm"]
            apply_chunks = []
            for dc in range(DC):
                w = [("mm", mm_bc2)] if dc == 0 else []
                emit("vector", lambda e, dc=dc: e.tensor_tensor(
                    scr[:, 0:1024], x_sb[:, dc * T:(dc + 1) * T], PS2[:, 0:1024], ALU.add), waits=w)
                emit("vector", lambda e, dc=dc: e.tensor_tensor(
                    ln_sb[:, dc * T:(dc + 1) * T], scr[:, 0:1024], PS2[:, 1024:2048], ALU.mult),
                    inc="dve")
                apply_chunks.append(c["dve"])
            return c["dve"], stats_mm[-1], apply_chunks

        def emit_ffn(apply_done, apply_chunks, C_prev_reader_mm, gu0_waits, wgu_load0_waits,
                     wgu_load1_waits, wd_load01_waits):
            gu_mm, silu_act, stt_dve = {}, {}, {}
            qmm, devac = {}, {}
            load_lg, qload_lg = {}, {}
            down_last_mm = {}
            NITER = 128

            def jc_of(n):
                it, j = divmod(n, 32)
                return (it // 2) * 32 + j

            def sched_wgu_load(n, extra):
                par = n % 2
                jcg = jc_of(n)
                emit("gpsimd", lambda e, jcg=jcg, par=par: e.dma_start(
                    out=wgu_sb[:, par * 4096:(par + 1) * 4096], in_=wgu[jcg]),
                    waits=extra, inc="lg", amt=16)
                load_lg[n] = c["lg"]

            sched_wgu_load(0, list(wgu_load0_waits))
            sched_wgu_load(1, list(wgu_load1_waits))

            for it in range(4):
                ihalf, thalf = divmod(it, 2)
                for j in range(32):
                    gidx = it * 32 + j
                    par = gidx % 2
                    jcg = ihalf * 32 + j
                    if gidx + 1 < NITER and (gidx + 1) not in load_lg:
                        sched_wgu_load(gidx + 1, [("mm", gu_mm[gidx - 1])] if gidx >= 1 else [])
                    w = [("lg", load_lg[gidx])]
                    if gidx >= 2:
                        w += [("act", silu_act[gidx - 2]), ("dve", stt_dve[gidx - 2]),
                              ("dve", apply_done)]
                    elif gu0_waits:
                        w += list(gu0_waits)

                    def gu_part(e, par=par, thalf=thalf, which=0, q4=0):
                        base = par * 4096 + (0 if which == 0 else 2048)
                        ups = PS[:, par * 1024 + which * 512:par * 1024 + (which + 1) * 512]
                        i = None
                        for kc in range(q4 * 4, q4 * 4 + 4):
                            i = e.matmul(ups, wgu_sb[:, base + kc * 128:base + (kc + 1) * 128],
                                         ln_sb[:, kc * T + thalf * TH:kc * T + thalf * TH + TH],
                                         start=(kc == 0), stop=(kc == DC - 1))
                        return i
                    if gidx < 2:
                        for which in (0, 1):
                            for q4 in range(4):
                                ww = list(w) + [("dve", apply_chunks[q4 * 4 + 3])]
                                last = which == 1 and q4 == 3
                                emit("tensor", lambda e, par=par, thalf=thalf, which=which, q4=q4:
                                     gu_part(e, par, thalf, which, q4),
                                     waits=ww, inc=("mm" if last else None))
                    else:
                        def gufn(e, par=par, thalf=thalf):
                            i = None
                            for which in (0, 1):
                                for q4 in range(4):
                                    i = gu_part(e, par, thalf, which, q4)
                            return i
                        emit("tensor", gufn, waits=w, inc="mm")
                    gu_mm[gidx] = c["mm"]
                    emit("scalar", lambda e, par=par, jcg=jcg: e.activation(
                        scr[:, par * 512:(par + 1) * 512], PS[:, par * 1024:par * 1024 + 512],
                        AF.Silu, scale=sg_sb[:, jcg:jcg + 1]),
                        waits=[("mm", gu_mm[gidx])], inc="act")
                    silu_act[gidx] = c["act"]
                    w = [("act", silu_act[gidx])]
                    if j == 0:
                        w += [("mm", C_prev_reader_mm if it == 0 else down_last_mm[it - 1])]
                    emit("vector", lambda e, par=par, jcg=jcg, j=j: e.scalar_tensor_tensor(
                        C[:, j * 512:(j + 1) * 512], PS[:, par * 1024 + 512:par * 1024 + 1024],
                        su_sb[:, jcg:jcg + 1], scr[:, par * 512:(par + 1) * 512],
                        ALU.mult, ALU.mult), waits=w, inc="dve")
                    stt_dve[gidx] = c["dve"]
                def sched_wd_load(qg, extra):
                    itq, rem = divmod(qg, 32)
                    mcq, qq = divmod(rem, 2)
                    ihq = itq // 2
                    parq = qg % 2
                    emit("gpsimd", lambda e, mcq=mcq, ihq=ihq, qq=qq, parq=parq: e.dma_start(
                        out=wdw[:, parq * 2048:(parq + 1) * 2048],
                        in_=wdt[mcq, :, ihq * 4096 + qq * 2048:ihq * 4096 + (qq + 1) * 2048]),
                        waits=extra, inc="lg", amt=16)
                    qload_lg[qg] = c["lg"]

                if it == 0:
                    sched_wd_load(0, list(wd_load01_waits))
                    sched_wd_load(1, list(wd_load01_waits))
                for mc in range(16):
                    dit = it * 16 + mc
                    dpar = mc % 2
                    for q in (0, 1):
                        qg = it * 32 + mc * 2 + q
                        if qg + 1 < (it + 1) * 32 + 2 and (qg + 1) not in qload_lg and qg + 1 < 128:
                            sched_wd_load(qg + 1, [("mm", qmm[qg - 1])] if qg >= 1 else [])
                        w = [("lg", qload_lg[qg]), ("dve", stt_dve[it * 32 + 31])]
                        if q == 0 and dit >= 2:
                            w += [("dve", devac[dit - 2])]

                        def dnfn(e, par2=qg % 2, q=q, dpar=dpar):
                            r = None
                            for kcl in range(16):
                                r = e.matmul(PS2[:, dpar * 512:(dpar + 1) * 512],
                                             wdw[:, par2 * 2048 + kcl * 128:par2 * 2048 + (kcl + 1) * 128],
                                             C[:, (q * 16 + kcl) * 512:(q * 16 + kcl + 1) * 512],
                                             start=(q == 0 and kcl == 0), stop=(q == 1 and kcl == 15))
                            return r
                        emit("tensor", dnfn, waits=w, inc="mm")
                        qmm[qg] = c["mm"]
                    emit("vector", lambda e, mc=mc, thalf=thalf, dpar=dpar: e.scalar_tensor_tensor(
                        x_sb[:, mc * T + thalf * TH:mc * T + thalf * TH + TH],
                        PS2[:, dpar * 512:(dpar + 1) * 512], sdh_sb[:, mc:mc + 1],
                        x_sb[:, mc * T + thalf * TH:mc * T + thalf * TH + TH],
                        ALU.mult, ALU.add), waits=[("mm", qmm[it * 32 + mc * 2 + 1])], inc="dve")
                    devac[dit] = c["dve"]
                down_last_mm[it] = qmm[it * 32 + 31]
            return dict(gu_last_mm=gu_mm[127], silu_last=silu_act[127], stt_last=stt_dve[127],
                        down_last_mm=down_last_mm[3], devac_last=devac[63], devac=devac)

        def emit_store(chunk_dve):
            for dc in range(DC):
                w = [("dve", chunk_dve(dc))]
                emit("sync", lambda e, dc=dc: e.dma_start(
                    out=yT[dc * 128:(dc + 1) * 128, :], in_=x_sb[:, dc * T:(dc + 1) * T]),
                    waits=w, inc="st", amt=16)
            emit("sync", lambda e: e.nop(), waits=[("st", c["st"])])

        def finish():
            block = ctx.enter_context(nc.Block())

            def mk(name):
                def body(e):
                    for waits, fn, trip in S_.ops[name]:
                        for (s, v) in waits:
                            e.wait_ge(sems[s], v)
                        inst = fn(e)
                        if trip is not None:
                            if inst is None:
                                inst = e.nop()
                            sname, amt = trip
                            inst.then_inc(sems[sname], amt)
                return body
            for name in ENGINES:
                getattr(block, name)(mk(name))
            return nc

        apply1, ln1_stats, applyc1 = emit_ln(
            x_ready=lambda dc: [("ls", 16 * (dc + 1))], ps_waits=[], ps2_waits=[], pool_extra=[])
        f1 = emit_ffn(apply_done=apply1, apply_chunks=applyc1, C_prev_reader_mm=0, gu0_waits=[],
                      wgu_load0_waits=[], wgu_load1_waits=[("mm", ln1_stats)],
                      wd_load01_waits=[])
        if upto == "ffn1":
            emit_store(lambda dc: f1["devac"][48 + dc])
            return finish()

        apply2, ln2_stats, applyc2 = emit_ln(
            x_ready=lambda dc: [("dve", f1["devac"][48 + dc])],
            ps_waits=[("act", f1["silu_last"]), ("dve", f1["stt_last"])],
            ps2_waits=[("dve", f1["devac_last"])],
            pool_extra=[("mm", f1["gu_last_mm"])])

        dbg_gate = []
        if dbg:
            emit("sync", lambda e: e.dma_start(out=dbg_ln2, in_=ln_sb[:]),
                 waits=[("dve", apply2)], inc="st", amt=16)
            dbg_gate.append(("st", c["st"]))

        qproj_mm, qrope_dve, wq_lg = [], [], []
        for oc in range(NH):
            par = oc % 2
            w = [("mm", f1["down_last_mm"])] if oc < 2 else [("mm", qproj_mm[oc - 2])]
            if oc == 0:
                w = w + [("lg", precast_lg)]
            emit("sync", lambda e, oc=oc, par=par: e.dma_start(
                out=wdw[:, par * 2048:(par + 1) * 2048], in_=wq_bf[oc]), waits=w, inc="ls", amt=16)
            wq_lg.append(c["ls"])
            w = [("ls", wq_lg[oc])]
            w += [("dve", qrope_dve[oc - 2])] if oc >= 2 else [("dve", f1["devac_last"])]
            if oc < 2:
                w += [("act", f1["silu_last"]), ("dve", f1["stt_last"])]
            else:
                w += [("dve", apply2)]

            def qp_part(e, par=par, nh=0, q4=0):
                r = None
                for kc in range(q4 * 4, q4 * 4 + 4):
                    r = e.matmul(PS[:, par * 1024 + nh * 512:par * 1024 + (nh + 1) * 512],
                                 wdw[:, par * 2048 + kc * 128:par * 2048 + (kc + 1) * 128],
                                 ln_sb[:, kc * T + nh * 512:kc * T + (nh + 1) * 512],
                                 start=(kc == 0), stop=(kc == DC - 1))
                return r
            if oc < 2:
                for nh in (0, 1):
                    for q4 in range(4):
                        ww = list(w) + [("dve", applyc2[q4 * 4 + 3])]
                        last = nh == 1 and q4 == 3
                        emit("tensor", lambda e, par=par, nh=nh, q4=q4: qp_part(e, par, nh, q4),
                             waits=ww, inc=("mm" if last else None))
            else:
                def projfn2(e, par=par):
                    r = None
                    for nh in (0, 1):
                        for q4 in range(4):
                            r = qp_part(e, par, nh, q4)
                    return r
                emit("tensor", projfn2, waits=w, inc="mm")
            qproj_mm.append(c["mm"])
            w = [("mm", qproj_mm[oc]), ("ls", misc_ls)]

            def qropefn(e, par=par, oc=oc):
                ps_ = PS[:, par * 1024:(par + 1) * 1024]
                e.tensor_tensor(scr[:, 0:1024], ps_, tbl_cos[:], ALU.mult)
                e.tensor_tensor(scr[0:64, 1024:2048], ps_[64:128, :], tbl_sin[0:64, :], ALU.mult)
                e.tensor_tensor(scr[64:128, 1024:2048], ps_[0:64, :], tbl_sin[64:128, :], ALU.mult)
                return e.tensor_tensor(C[:, oc * 1024:(oc + 1) * 1024],
                                       scr[:, 0:1024], scr[:, 1024:2048], ALU.add)
            emit("vector", qropefn, waits=w, inc="dve")
            qrope_dve.append(c["dve"])

        emit("sync", lambda e: e.dma_start(out=tbl_cos[:], in_=cos2k_d),
             waits=[("dve", qrope_dve[-1])], inc="ls", amt=16)
        emit("sync", lambda e: e.dma_start(out=tbl_sin[:], in_=sinSk_d), inc="ls", amt=16)
        tblk_ls = c["ls"]
        kproj_mm, krope_dve, wk_lg = [], [], []
        for oc in range(NKV):
            par = oc % 2
            w = [("mm", qproj_mm[-1])] if oc < 2 else [("mm", kproj_mm[oc - 2])]
            emit("gpsimd", lambda e, oc=oc, par=par: e.dma_start(
                out=wdw[:, par * 2048:(par + 1) * 2048], in_=wkt[oc]), waits=w, inc="lg", amt=16)
            wk_lg.append(c["lg"])
            w = [("lg", wk_lg[oc])]
            w += [("dve", krope_dve[oc - 2])] if oc >= 2 else [("dve", qrope_dve[-1])]

            def projfn(e, par=par):
                r = None
                for nh in (0, 1):
                    for kc in range(DC):
                        r = e.matmul(PS[:, par * 1024 + nh * 512:par * 1024 + (nh + 1) * 512],
                                     wdw[:, par * 2048 + kc * 128:par * 2048 + (kc + 1) * 128],
                                     ln_sb[:, kc * T + nh * 512:kc * T + (nh + 1) * 512],
                                     start=(kc == 0), stop=(kc == DC - 1))
                return r
            emit("tensor", projfn, waits=w, inc="mm")
            kproj_mm.append(c["mm"])
            w = [("mm", kproj_mm[oc])]
            if oc == 0:
                w += [("ls", tblk_ls)]

            def ropefn(e, par=par, oc=oc):
                ps_ = PS[:, par * 1024:(par + 1) * 1024]
                e.tensor_tensor(scr[:, 0:1024], ps_, tbl_cos[:], ALU.mult)
                e.tensor_tensor(scr[0:64, 1024:2048], ps_[64:128, :], tbl_sin[0:64, :], ALU.mult)
                e.tensor_tensor(scr[64:128, 1024:2048], ps_[0:64, :], tbl_sin[64:128, :], ALU.mult)
                return e.tensor_tensor(kvK[:, oc * 1024:(oc + 1) * 1024],
                                       scr[:, 0:1024], scr[:, 1024:2048], ALU.add)
            emit("vector", ropefn, waits=w, inc="dve")
            krope_dve.append(c["dve"])

        vproj_mm, vevac_act = [], []
        for dhalf in (0, 1):
            wv_w = [("mm", kproj_mm[-1])] if dhalf == 0 else [("mm", vproj_mm[-1])]
            emit("gpsimd", lambda e, dhalf=dhalf: e.dma_start(
                out=wdw[:].rearrange("p (k o) -> p k o", o=256),
                in_=wvr[:, :, dhalf * 256:(dhalf + 1) * 256].rearrange("k p o -> p k o")),
                waits=wv_w, inc="lg", amt=16)
            wv_lg = c["lg"]
            for tc in range(8):
                vit = dhalf * 8 + tc
                par = vit % 2
                w = [("lg", wv_lg), ("dve", apply2)]
                if vit >= 2:
                    w += [("act", vevac_act[vit - 2])]
                else:
                    w += [("dve", f1["devac_last"])]

                def vpfn(e, par=par, tc=tc):
                    r = None
                    for kc in range(DC):
                        r = e.matmul(PS2[:, par * 512:par * 512 + 256],
                                     ln_sb[:, kc * T + tc * 128:kc * T + (tc + 1) * 128],
                                     wdw[:, kc * 256:(kc + 1) * 256],
                                     start=(kc == 0), stop=(kc == DC - 1))
                    return r
                emit("tensor", vpfn, waits=w, inc="mm")
                vproj_mm.append(c["mm"])
                emit("scalar", lambda e, par=par, tc=tc, dhalf=dhalf: e.activation(
                    kvK[:, 4096 + tc * 512 + dhalf * 256:4096 + tc * 512 + dhalf * 256 + 256],
                    PS2[:, par * 512:par * 512 + 256], AF.Copy),
                    waits=[("mm", vproj_mm[-1])], inc="act")
                vevac_act.append(c["act"])

        stage_w = [("act", vevac_act[-1]), ("dve", krope_dve[-1])]
        emit("gpsimd", lambda e: e.dma_start(
            out=kv_in[0:512, :].rearrange("(cc p) t -> p cc t", p=128),
            in_=kvK[:, 0:4096].rearrange("p (cc t) -> p cc t", t=1024)),
            waits=stage_w, inc="lg", amt=16)
        vbytes = kv_in[512:1024, :].rearrange("r2 (h d) -> (r2 h) d", d=512)
        emit("gpsimd", lambda e: e.dma_start(
            out=vbytes.rearrange("(cc p) d -> p cc d", p=128),
            in_=kvK[:, 4096:8192].rearrange("p (cc d) -> p cc d", d=512)),
            inc="lg", amt=16)
        kvout_lg = c["lg"]
        emit("gpsimd", lambda e: e.collective_compute(
            "AllGather", ALU.bypass, replica_groups=[list(range(NCORES))],
            ins=[kv_in.opt()], outs=[kv_ag.opt()]),
            waits=[("lg", kvout_lg)], inc="cc")
        cc_done = c["cc"]
        if dbg:
            emit("gpsimd", lambda e: e.dma_start(out=dbg_kvag, in_=kv_ag),
                 waits=[("cc", cc_done)], inc="lg", amt=16)
            dbg_gate.append(("lg", c["lg"]))
            emit("sync", lambda e: e.dma_start(out=dbg_q, in_=C[:]),
                 waits=[("dve", qrope_dve[-1])], inc="st", amt=16)
            dbg_gate.append(("st", c["st"]))
            emit("sync", lambda e: e.dma_start(out=dbg_kvst, in_=kvK[:]),
                 waits=list(stage_w), inc="st", amt=16)
            dbg_gate.append(("st", c["st"]))

        probsT = wgu_sb
        pq1 = wgu_sb[:, 4096:4608]
        pq2 = wgu_sb[:, 4608:5120]
        qq = [wgu_sb[:, 5120:5632], wgu_sb[:, 5632:6144]]
        oo = [wgu_sb[:, 6144:6656], wgu_sb[:, 6656:7168]]
        ssq = [wgu_sb[:, 7168:7680], wgu_sb[:, 7680:8192]]
        exp_act, scpair_mm, pv_mm, oevac_dve = [], [], [], []
        padd_dve, qadd_dve, oadd_dve, sadd_dve, rs_mm = [], [], [], [], []
        last_tile_mm = []
        last_sc_of_g, last_pv_of_g = {}, {}
        k_ls, v_ls = [], []
        gpair = 0
        for g in range(NKV):
            w = [("cc", cc_done), ("lg", kvout_lg)]
            if g >= 1:
                w += [("mm", last_sc_of_g[g - 1])]
            emit("sync", lambda e, g=g: e.dma_start(
                out=kvK[:].rearrange("p (r t) -> p r t", t=1024),
                in_=kv_ag.rearrange("(r x) t -> r x t", x=1024)[:, g * 128:(g + 1) * 128, :].rearrange("r p t -> p r t")),
                waits=w, inc="ls", amt=16)
            k_ls.append(c["ls"])
            w2 = [("mm", last_pv_of_g[g - 1])] if g >= 1 else []
            for r in range(8):
                vsrc = kv_ag[r * 1024 + 512:(r + 1) * 1024, :].rearrange(
                    "r2 (h d) -> (r2 h) d", d=512)[:, g * 128:(g + 1) * 128]
                emit("sync", lambda e, r=r, vsrc=vsrc: e.dma_start(
                    out=vg[:, r * 1024:(r + 1) * 1024].rearrange("p (s d) -> p s d", d=128),
                    in_=vsrc.rearrange("(s p) d -> p s d", p=128)),
                    waits=(w2 if r == 0 else []), inc="ls", amt=16)
            v_ls.append(c["ls"])
            for h in range(4):
                hh = g * 4 + h
                for qcg in (0, 1):
                    tile = hh * 2 + qcg
                    tpar = tile % 2
                    Ops = PS2[:, tpar * 1024:tpar * 1024 + 512]
                    RSps = PS2[:, tpar * 1024 + 512:tpar * 1024 + 1024]
                    tile_base_pair = gpair
                    tile_q_base = len(qadd_dve)
                    tile_o_base = len(oadd_dve)
                    tile_s_base = len(sadd_dve)
                    first_tile_of_g = (h == 0 and qcg == 0)

                    def scfn(e, p, hh=hh, qcg=qcg):
                        r = None
                        for k in (0, 1):
                            cch = 2 * p + k
                            slot = cch % 4
                            r = e.matmul(PS[:, slot * 512:(slot + 1) * 512],
                                         kvK[:, cch * 128:(cch + 1) * 128],
                                         C[:, hh * 1024 + qcg * 512:hh * 1024 + qcg * 512 + 512],
                                         start=True, stop=True)
                        return r
                    w = [("ls", k_ls[g])]
                    if gpair >= 2:
                        w += [("act", exp_act[gpair - 2])]
                    else:
                        w += [("dve", krope_dve[-1])]
                    emit("tensor", lambda e, scfn=scfn: scfn(e, 0), waits=w, inc="mm")
                    scpair_mm.append(c["mm"])
                    gpair += 1
                    for p in range(32):
                        gp = tile_base_pair + p
                        w = [("mm", scpair_mm[gp])]
                        if gp >= 4:
                            w += [("mm", pv_mm[gp - 4]), ("dve", padd_dve[gp - 4])]

                        def expfn(e, p=p, gp=gp):
                            return e.activation(probsT[:, (gp % 4) * 1024:(gp % 4 + 1) * 1024],
                                                PS[:, (p % 2) * 1024:(p % 2 + 1) * 1024], AF.Exp)
                        emit("scalar", expfn, waits=w, inc="act")
                        exp_act.append(c["act"])
                        def paddfn(e, p=p, gp=gp):
                            half = gp % 4
                            dst = pq1 if p % 2 == 0 else pq2
                            return e.tensor_tensor(dst, probsT[:, half * 1024:half * 1024 + 512],
                                                   probsT[:, half * 1024 + 512:(half + 1) * 1024],
                                                   ALU.add)
                        emit("vector", paddfn, waits=[("act", exp_act[gp])], inc="dve")
                        padd_dve.append(c["dve"])
                        if p % 2 == 1:
                            emit("vector", lambda e, j=(p // 2) % 2: e.tensor_tensor(
                                qq[j], pq1, pq2, ALU.add), inc="dve")
                            qadd_dve.append(c["dve"])
                        if p % 4 == 3:
                            j8 = p // 4
                            emit("vector", lambda e, j8=j8: e.tensor_tensor(
                                oo[j8 % 2], qq[0], qq[1], ALU.add), inc="dve")
                            oadd_dve.append(c["dve"])
                        if p % 8 == 7:
                            j16 = p // 8
                            s_glob = len(sadd_dve)
                            ws_ = [("mm", rs_mm[s_glob - 2])] if s_glob >= 2 else []
                            emit("vector", lambda e, j16=j16: e.tensor_tensor(
                                ssq[j16 % 2], oo[0], oo[1], ALU.add), waits=ws_, inc="dve")
                            sadd_dve.append(c["dve"])
                        if p + 1 < 32:
                            w = []
                            if gpair >= 2:
                                w += [("act", exp_act[gpair - 2])]
                            else:
                                w += [("dve", krope_dve[-1])]
                            emit("tensor", lambda e, scfn=scfn, p=p: scfn(e, p + 1), waits=w, inc="mm")
                            scpair_mm.append(c["mm"])
                            gpair += 1
                        w = [("act", exp_act[gp])]
                        if p == 0:
                            if tile >= 2:
                                w += [("dve", oevac_dve[tile - 2])]
                            else:
                                w += [("act", vevac_act[-1])]
                            if first_tile_of_g:
                                w += [("ls", v_ls[g])]

                        def pvfn(e, p=p, gp=gp, Ops=Ops):
                            r = None
                            for k in (0, 1):
                                cch = 2 * p + k
                                slot = 2 * (gp % 4) + k
                                r = e.matmul(Ops, vg[:, cch * 128:(cch + 1) * 128],
                                             probsT[:, slot * 512:(slot + 1) * 512],
                                             start=(cch == 0), stop=(cch == 63))
                            return r
                        emit("tensor", pvfn, waits=w, inc="mm")
                        pv_mm.append(c["mm"])
                        if (p % 8 == 0 and p >= 8) or p == 31:
                            for j16 in ([(p - 8) // 8] if p != 31 else [3]):
                                emit("tensor", lambda e, j16=j16, RSps=RSps: e.matmul(
                                    RSps, ones_b[:], ssq[j16 % 2],
                                    start=(j16 == 0), stop=(j16 == 3)),
                                    waits=[("dve", sadd_dve[tile_s_base + j16])], inc="mm")
                                rs_mm.append(c["mm"])
                    last_sc_of_g[g] = scpair_mm[-1]
                    last_pv_of_g[g] = pv_mm[-1]
                    last_tile_mm.append(rs_mm[-1])

                    def oefn(e, Ops=Ops, RSps=RSps, tpar=tpar, hh=hh, qcg=qcg):
                        e.reciprocal(scr[:, tpar * 512:(tpar + 1) * 512], RSps)
                        return e.tensor_tensor(
                            ln_sb[:, hh * 1024 + qcg * 512:hh * 1024 + qcg * 512 + 512],
                            Ops, scr[:, tpar * 512:(tpar + 1) * 512], ALU.mult)
                    wv_ = [("mm", rs_mm[-1])] + (dbg_gate if (dbg and tile == 0) else [])
                    emit("vector", oefn, waits=wv_, inc="dve")
                    oevac_dve.append(c["dve"])

        wo_mm, woevac_dve, wo_lg = [], [], []
        for oc in range(NH):
            par = oc % 2
            w = [("mm", vproj_mm[-1])] if oc < 2 else [("mm", wo_mm[2 * (oc - 2) + 1])]
            emit("sync", lambda e, oc=oc, par=par: e.dma_start(
                out=wdw[:, par * 2048:(par + 1) * 2048], in_=wo_bf[oc]), waits=w, inc="ls", amt=16)
            wo_lg.append(c["ls"])
            for th in (0, 1):
                woit = oc * 2 + th
                wpar = woit % 4
                w = [("ls", wo_lg[oc]), ("dve", oevac_dve[-1])]
                if woit < 4:
                    w += [("act", exp_act[-1])]
                else:
                    w += [("dve", woevac_dve[woit - 4])]

                def wofn(e, par=par, th=th, wpar=wpar):
                    r = None
                    for hc in range(DC):
                        r = e.matmul(PS[:, wpar * 512:(wpar + 1) * 512],
                                     wdw[:, par * 2048 + hc * 128:par * 2048 + (hc + 1) * 128],
                                     ln_sb[:, hc * T + th * TH:hc * T + th * TH + TH],
                                     start=(hc == 0), stop=(hc == DC - 1))
                    return r
                emit("tensor", wofn, waits=w, inc="mm")
                wo_mm.append(c["mm"])
                emit("vector", lambda e, oc=oc, th=th, wpar=wpar: e.tensor_tensor(
                    x_sb[:, oc * T + th * TH:oc * T + th * TH + TH],
                    x_sb[:, oc * T + th * TH:oc * T + th * TH + TH],
                    PS[:, wpar * 512:(wpar + 1) * 512], ALU.add),
                    waits=[("mm", wo_mm[-1])], inc="dve")
                woevac_dve.append(c["dve"])

        if upto == "attn":
            emit_store(lambda dc: woevac_dve[2 * dc + 1])
            return finish()

        apply3, ln3_stats, applyc3 = emit_ln(
            x_ready=lambda dc: [("dve", woevac_dve[2 * dc + 1])],
            ps_waits=[("dve", woevac_dve[-1])],
            ps2_waits=[("dve", oevac_dve[-1])],
            pool_extra=[("mm", ln2_stats)])
        f2 = emit_ffn(apply_done=apply3, apply_chunks=applyc3, C_prev_reader_mm=scpair_mm[-1],
                      gu0_waits=[("dve", woevac_dve[-1]), ("mm", last_tile_mm[-1])],
                      wgu_load0_waits=[("mm", last_tile_mm[-1])],
                      wgu_load1_waits=[("mm", ln3_stats)],
                      wd_load01_waits=[("mm", wo_mm[-1])])

        emit_store(lambda dc: f2["devac"][48 + dc])
        return finish()



def _prep_weights(Wq, Wk, Wv, Wo, Wg, Wu, Wd):
    wgu_h = np.concatenate([
        np.ascontiguousarray(Wg.reshape(16, 128, 64, 128).transpose(2, 1, 0, 3)).reshape(64, 128, 2048),
        np.ascontiguousarray(Wu.reshape(16, 128, 64, 128).transpose(2, 1, 0, 3)).reshape(64, 128, 2048),
    ], axis=2)
    wgu_h = np.ascontiguousarray(wgu_h)
    wdt_h = np.ascontiguousarray(
        Wd.reshape(64, 128, 16, 128).transpose(2, 1, 0, 3)).reshape(16, 128, 8192).copy()
    wqt_h = np.ascontiguousarray(
        Wq.reshape(16, 128, 16, 128).transpose(0, 3, 2, 1)).reshape(16, 128, 2048).copy()
    wkt_h = np.ascontiguousarray(
        Wk.reshape(4, 128, 16, 128).transpose(0, 3, 2, 1)).reshape(4, 128, 2048).copy()
    wvr_h = np.ascontiguousarray(Wv.T).reshape(16, 128, 512).copy()
    wot_h = np.ascontiguousarray(
        Wo.reshape(16, 128, 16, 128).transpose(0, 3, 2, 1)).reshape(16, 128, 2048).copy()
    return wgu_h, wdt_h, wqt_h, wkt_h, wvr_h, wot_h


def _rope_tables(core):
    pos = (np.arange(T, dtype=np.float32) + np.float32(core * T))
    inv = (1.0 / (np.float32(10000.0) ** (np.arange(0, HD, 2, dtype=np.float32)
                                          / np.float32(HD)))).astype(np.float32)
    freqs = pos[:, None] * inv[None, :]
    cos = np.cos(freqs).astype(np.float32)
    sin = np.sin(freqs).astype(np.float32)
    cos2 = np.ascontiguousarray(np.concatenate([cos.T, cos.T], axis=0))
    sinS = np.ascontiguousarray(np.concatenate([-sin.T, sin.T], axis=0))
    sc = np.float32(1.0 / math.sqrt(HD))
    return cos2, sinS, np.ascontiguousarray(cos2 * sc), np.ascontiguousarray(sinS * sc)


def run_device(inputs, upto="full"):
    x = np.asarray(inputs["L0_attn_hidden_states"], dtype=np.float32)
    wgu_h, wdt_h, wqt_h, wkt_h, wvr_h, wot_h = _prep_weights(
        np.asarray(inputs["L0_attn_Wq"], np.float32), np.asarray(inputs["L0_attn_Wk"], np.float32),
        np.asarray(inputs["L0_attn_Wv"], np.float32), np.asarray(inputs["L0_attn_Wo"], np.float32),
        np.asarray(inputs["L0_ffn_w_gate"], np.int8), np.asarray(inputs["L0_ffn_w_up"], np.int8),
        np.asarray(inputs["L0_ffn_w_down"], np.int8))
    sgl = np.ascontiguousarray(np.asarray(inputs["L0_ffn_s_gate"], np.float32).reshape(64, 128).T)
    sul = np.ascontiguousarray(np.asarray(inputs["L0_ffn_s_up"], np.float32).reshape(64, 128).T)
    sdl = np.ascontiguousarray(np.asarray(inputs["L0_ffn_s_down"], np.float32).reshape(16, 128).T)

    xTfull = np.ascontiguousarray(x[0].T)
    nc = build_program(upto)
    in_maps = []
    for i in range(NCORES):
        cos2k, sinSk, cos2q, sinSq = _rope_tables(i)
        in_maps.append({
            "xT": np.ascontiguousarray(xTfull[:, i * T:(i + 1) * T]),
            "wgu": wgu_h, "wdt": wdt_h, "wqt": wqt_h, "wkt": wkt_h, "wvr": wvr_h, "wot": wot_h,
            "sgd": sgl, "sud": sul, "sdd": sdl,
            "cos2k": cos2k, "sinSk": sinSk, "cos2q": cos2q, "sinSq": sinSq,
        })
    res = run_bass_kernel_spmd(nc, in_maps, core_ids=list(range(NCORES)))
    outT = np.concatenate([np.asarray(r["yT"]) for r in res.results], axis=1)
    return np.ascontiguousarray(outT.T)[None].astype(np.float32)


def kernel(**inputs):
    return run_device(inputs, "full")
